# version 01
# baseline (speedup 1.0000x reference)
import contextlib
import numpy as np
import ml_dtypes
import concourse.bass as bass
import concourse.mybir as mybir
from concourse.bass_utils import run_bass_kernel_spmd

F32 = mybir.dt.float32
BF16 = mybir.dt.bfloat16
I32 = mybir.dt.int32
AF = mybir.ActivationFunctionType
ALU = mybir.AluOpType
AXX = mybir.AxisListType.X
BF = ml_dtypes.bfloat16

NCORES = 8
D = 2048
PC = 2048
NTILES = PC // 128
B, S = 4, 4096
SEQ = 4096
import os
BIG = int(os.environ.get("KBIG", "512"))

_BUILT = {}


class Ctr:
    def __init__(self, sem, step):
        self.sem = sem
        self.step = step
        self.n = 0

    def bump(self, k=1):
        self.n += k * self.step
        return self.n


class _Step:
    __slots__ = ("eng", "fn", "r", "w", "dma", "fsz", "idx", "need_inc",
                 "val", "deps", "ctr")


class Sched:
    ENGINES = ("sync", "tensor", "vector", "scalar", "gpsimd")

    def __init__(self):
        self.nc = bass.Bass("TRN2", target_bir_lowering=False)
        self.es = contextlib.ExitStack()
        self.steps = []

    def sb(self, name, shape, dt):
        return self.es.enter_context(self.nc.sbuf_tensor(name, shape, dt))

    def ps(self, name, shape, dt):
        return self.es.enter_context(self.nc.psum_tensor(name, shape, dt))

    def din(self, name, shape, dt):
        return self.nc.dram_tensor(name, shape, dt, kind="ExternalInput").ap()

    def dout(self, name, shape, dt):
        return self.nc.dram_tensor(name, shape, dt, kind="ExternalOutput").ap()

    def const_tile(self, name, value, dt=F32):
        tile = self.sb(name, [128, 1], dt)
        self.step("vector", lambda v, tile=tile, value=value:
                  v.memset(tile[:], value), w=[tile])
        return tile

    def step(self, eng, fn, r=(), w=(), dma=False, fsz=0):
        s = _Step()
        s.eng, s.fn, s.dma, s.fsz = eng, fn, dma, fsz
        s.r = [id(t) for t in r]
        s.w = [id(t) for t in w]
        s.idx = len(self.steps)
        s.need_inc = dma
        s.val = None
        s.deps = []
        self.steps.append(s)
        return s

    def finalize(self):
        last_writer = {}
        readers = {}
        prev_dma = {}
        for s in self.steps:
            deps = {}
            if s.dma:
                pd = prev_dma.get(s.eng)
                if pd is not None:
                    deps[pd.idx] = pd
                prev_dma[s.eng] = s
            for t in s.r + s.w:
                lw = last_writer.get(t)
                if lw is not None and lw is not s:
                    deps[lw.idx] = lw
            for t in s.w:
                for rd in readers.get(t, ()):
                    if rd is not s:
                        deps[rd.idx] = rd
            for d in deps.values():
                if d.eng == s.eng and not d.dma and not s.dma:
                    if d.eng == "tensor" or d.fsz >= BIG:
                        continue
                d.need_inc = True
                s.deps.append(d)
            for t in s.r:
                readers.setdefault(t, []).append(s)
            for t in s.w:
                last_writer[t] = s
                readers[t] = []
        ctr = {}
        for e in self.ENGINES:
            ctr[(e, False)] = Ctr(
                self.es.enter_context(self.nc.semaphore(f"c_{e}")), 1)
            ctr[(e, True)] = Ctr(
                self.es.enter_context(self.nc.semaphore(f"d_{e}")), 16)
        for s in self.steps:
            if s.need_inc:
                c = ctr[(s.eng, s.dma)]
                s.val = c.bump(1)
                s.ctr = c
        qs = {e: [] for e in self.ENGINES}
        dma_totals = {e: 0 for e in self.ENGINES}
        for s in self.steps:
            waits = {}
            for d in s.deps:
                key = (d.eng, d.dma)
                waits[key] = max(waits.get(key, 0), d.val)
            wl = [(ctr[k].sem, v) for k, v in waits.items()]
            inc = ctr[(s.eng, s.dma)] if s.need_inc else None

            def run(engobj, s=s, wl=wl, inc=inc):
                for sem, v in wl:
                    engobj.wait_ge(sem, v)
                inst = s.fn(engobj)
                if inc is not None:
                    insts = inst if isinstance(inst, (list, tuple)) else [inst]
                    insts[-1].then_inc(inc.sem, inc.step)

            qs[s.eng].append(run)
            if s.dma:
                dma_totals[s.eng] = s.val
        final_waits = [(ctr[(e, True)].sem, v) for e, v in dma_totals.items()
                       if v]

        def final(engobj, fw=final_waits):
            for sem, v in fw:
                engobj.wait_ge(sem, v)

        qs["gpsimd"].append(final)
        with self.nc.Block() as block:
            for e in self.ENGINES:
                def body(engobj, e=e):
                    for rn in qs[e]:
                        rn(engobj)
                getattr(block, e)(body)
        self.es.close()
        return self.nc



def dma(S, eng, out_view, in_view, r=(), w=()):
    return S.step(eng, lambda e: e.dma_start(out=out_view, in_=in_view),
                  r=list(r), w=list(w), dma=True)


def emit_transpose(S, src_views, dst_views, ptp, ident, extra_r=(),
                   ptp_owner=None):
    if ptp_owner is None:
        ptp_owner = ptp
    n = len(src_views)
    for lo in range(0, n, 8):
        hi = min(lo + 8, n)

        def tp(t, lo=lo, hi=hi):
            last = None
            for j in range(lo, hi):
                last = t.transpose(ptp[:, 128 * (j - lo):128 * (j - lo + 1)],
                                   src_views[j][0], ident[:])
            return last
        S.step("tensor", tp,
               r=[v[1] for v in src_views[lo:hi]] + [ident] + list(extra_r),
               w=[ptp_owner])

        def ev(v, lo=lo, hi=hi):
            last = None
            for j in range(lo, hi):
                last = v.tensor_copy(dst_views[j][0],
                                     ptp[:, 128 * (j - lo):128 * (j - lo + 1)])
            return last
        S.step("vector", ev, r=[ptp_owner],
               w=[v[1] for v in dst_views[lo:hi]], fsz=BIG)


def emit_ln_rows(S, xin_view, xin_h, xn_out_view, xn_out_h, tmp, stats):
    return emit_ln_rows_v(S, xin_view, xin_h, xn_out_view, xn_out_h,
                          tmp[:], tmp, stats)


def emit_ln_rows_v(S, xin_view, xin_h, xn_out_view, xn_out_h, tmp_view,
                   tmp_h, stats):
    S.step("vector", lambda v: v.reduce_sum(stats["sumx"][:], xin_view,
                                            axis=AXX),
           r=[xin_h], w=[stats["sumx"]])
    S.step("scalar", lambda s: s.activation(out=tmp_view, in_=xin_view,
                                            func=AF.Square,
                                            accum_out=stats["sumsq"][:]),
           r=[xin_h], w=[tmp_h, stats["sumsq"]])
    S.step("vector", lambda v: v.tensor_scalar(
        out=stats["mu"][:], in0=stats["sumx"][:], scalar1=1.0 / D,
        scalar2=None, op0=ALU.mult), r=[stats["sumx"]], w=[stats["mu"]])
    S.step("vector", lambda v: v.tensor_scalar(
        out=stats["ex2"][:], in0=stats["sumsq"][:], scalar1=1.0 / D,
        scalar2=None, op0=ALU.mult), r=[stats["sumsq"]], w=[stats["ex2"]])
    S.step("vector", lambda v: v.tensor_tensor(
        out=stats["mu2"][:], in0=stats["mu"][:], in1=stats["mu"][:],
        op=ALU.mult), r=[stats["mu"]], w=[stats["mu2"]])
    S.step("vector", lambda v: v.tensor_tensor(
        out=stats["var"][:], in0=stats["ex2"][:], in1=stats["mu2"][:],
        op=ALU.subtract), r=[stats["ex2"], stats["mu2"]], w=[stats["var"]])
    S.step("scalar", lambda s: s.activation(
        out=stats["sd"][:], in_=stats["var"][:], func=AF.Sqrt,
        bias=stats["eps"][:]), r=[stats["var"], stats["eps"]], w=[stats["sd"]])
    S.step("vector", lambda v: v.reciprocal(stats["rinv"][:], stats["sd"][:]),
           r=[stats["sd"]], w=[stats["rinv"]])
    S.step("vector", lambda v: v.tensor_scalar(
        out=xn_out_view, in0=xin_view, scalar1=stats["mu"][:],
        scalar2=stats["rinv"][:], op0=ALU.subtract, op1=ALU.mult),
        r=[xin_h, stats["mu"], stats["rinv"]], w=[xn_out_h])


def ln_stats_tiles(S, pfx):
    d = {k: S.sb(f"{pfx}_{k}", [128, 1], F32)
         for k in ("sumx", "sumsq", "mu", "mu2", "ex2", "var", "sd", "rinv")}
    d["eps"] = S.const_tile(f"{pfx}_eps", 1e-5)
    return d



def build_ln():
    S = Sched()
    x = S.din("x", [PC, D], F32)
    xn = S.dout("xn", [PC, D], BF16)
    xin = [S.sb(f"xin{i}", [128, D], F32) for i in range(2)]
    xout = [S.sb(f"xout{i}", [128, D], BF16) for i in range(2)]
    tmp = S.sb("tmp", [128, D], BF16)
    stats = ln_stats_tiles(S, "st")
    OUT = object()
    for t in range(NTILES):
        b = t % 2
        dma(S, "sync", xin[b][:], x[128 * t:128 * (t + 1), :], w=[xin[b]])
        emit_ln_rows(S, xin[b][:], xin[b], xout[b][:], xout[b], tmp, stats)
        dma(S, "gpsimd", xn[128 * t:128 * (t + 1), :], xout[b][:],
            r=[xout[b]], w=[OUT])
    return S.finalize()


QKV_OC = 2560

def build_qkv0():
    S = Sched()
    xn = S.din("xn", [PC, D], BF16)
    wqkv = S.din("wqkv", [D, QKV_OC], F32)
    cost = S.din("cost", [PC, 64], F32)
    sint = S.din("sint", [PC, 64], F32)
    identb = S.din("identb", [128, 128], BF16)
    q_o = S.dout("q", [PC, D], BF16)
    k_o = S.dout("k", [PC, 256], BF16)
    v_o = S.dout("v", [PC, 256], BF16)

    wstg = [S.sb(f"wstg{i}", [128, QKV_OC], F32) for i in range(2)]
    wb = S.sb("wb", [128, 16, QKV_OC], BF16)
    sident = S.sb("sident", [128, 128], BF16)
    scos = S.sb("scos", [128, NTILES, 64], F32)
    ssin = S.sb("ssin", [128, NTILES, 64], F32)
    xtile = [S.sb(f"xtile{i}", [128, D], BF16) for i in range(2)]
    xnT = [S.sb(f"xnT{i}", [128, 16, 128], BF16) for i in range(2)]
    otile = [S.sb(f"otile{i}", [128, QKV_OC], BF16) for i in range(2)]
    ta = S.sb("ta", [128, 1024], F32)
    tb = S.sb("tb", [128, 1024], F32)
    ptp = S.ps("ptp", [128, 1024], BF16)
    pmm = S.ps("pmm", [128, 512], F32)
    OUT = object()

    dma(S, "sync", sident[:], identb, w=[sident])
    dma(S, "sync", scos[:], cost.rearrange("(t p) f -> p t f", p=128),
        w=[scos])
    dma(S, "sync", ssin[:], sint.rearrange("(t p) f -> p t f", p=128),
        w=[ssin])
    for kc in range(16):
        b = kc % 2
        dma(S, "sync", wstg[b][:], wqkv[128 * kc:128 * (kc + 1), :],
            w=[wstg[b]])
        S.step("vector", lambda v, kc=kc, b=b: v.tensor_copy(wb[:, kc, :],
                                                             wstg[b][:]),
               r=[wstg[b]], w=[wb], fsz=BIG)

    for t in range(NTILES):
        b = t % 2
        dma(S, "sync", xtile[b][:], xn[128 * t:128 * (t + 1), :], w=[xtile[b]])
        emit_transpose(S,
                       [(xtile[b][:, 128 * j:128 * (j + 1)], xtile[b])
                        for j in range(16)],
                       [(xnT[b][:, j, :], xnT[b]) for j in range(16)],
                       ptp, sident)
        for oc in range(5):
            def mms(tt, b=b, oc=oc):
                last = None
                for kc in range(16):
                    last = tt.matmul(pmm[:], xnT[b][:, kc, :],
                                     wb[:, kc, 512 * oc:512 * (oc + 1)],
                                     start=(kc == 0), stop=(kc == 15))
                return last
            S.step("tensor", mms, r=[xnT[b], wb], w=[pmm])
            if oc < 4:
                emit_rope_evac(S, pmm, otile[b], 512 * oc, 4, scos, ssin, t,
                               ta, tb)
            else:
                emit_rope_evac(S, pmm, otile[b], 2048, 2, scos, ssin, t,
                               ta, tb, kv=True)
        dma(S, "gpsimd", q_o[128 * t:128 * (t + 1), :], otile[b][:, 0:2048],
            r=[otile[b]], w=[OUT])
        dma(S, "gpsimd", k_o[128 * t:128 * (t + 1), :], otile[b][:, 2048:2304],
            r=[otile[b]], w=[OUT])
        dma(S, "gpsimd", v_o[128 * t:128 * (t + 1), :], otile[b][:, 2304:2560],
            r=[otile[b]], w=[OUT])
    return S.finalize()


def emit_rope_evac(S, pmm, otile, col0, nh, scos, ssin, t, ta, tb, kv=False):
    pq = pmm[:, 0:128 * nh].rearrange("p (h two d) -> p h two d", two=2, d=64)
    q1, q2 = pq[:, :, 0, :], pq[:, :, 1, :]
    cosb = scos[:, t, :].unsqueeze(1).broadcast_to([128, nh, 64])
    sinb = ssin[:, t, :].unsqueeze(1).broadcast_to([128, nh, 64])
    va = ta[:].rearrange("p (h d) -> p h d", d=64)[:, 0:nh, :]
    vb = tb[:].rearrange("p (h d) -> p h d", d=64)[:, 0:nh, :]
    ot = otile[:, col0:col0 + 128 * nh].rearrange("p (h two d) -> p h two d",
                                                  two=2, d=64)
    fs = nh * 64
    S.step("vector", lambda v: v.tensor_tensor(out=va, in0=q1, in1=cosb,
                                               op=ALU.mult),
           r=[pmm, scos], w=[ta], fsz=fs)
    S.step("vector", lambda v: v.tensor_tensor(out=vb, in0=q2, in1=sinb,
                                               op=ALU.mult),
           r=[pmm, ssin], w=[tb], fsz=fs)
    S.step("vector", lambda v: v.tensor_tensor(out=ot[:, :, 0, :], in0=va,
                                               in1=vb, op=ALU.subtract),
           r=[ta, tb], w=[otile], fsz=fs)
    S.step("vector", lambda v: v.tensor_tensor(out=va, in0=q1, in1=sinb,
                                               op=ALU.mult),
           r=[pmm, ssin], w=[ta], fsz=fs)
    S.step("vector", lambda v: v.tensor_tensor(out=vb, in0=q2, in1=cosb,
                                               op=ALU.mult),
           r=[pmm, scos], w=[tb], fsz=fs)
    S.step("vector", lambda v: v.tensor_tensor(out=ot[:, :, 1, :], in0=va,
                                               in1=vb, op=ALU.add),
           r=[ta, tb], w=[otile], fsz=fs)
    if kv:
        S.step("vector", lambda v: v.tensor_copy(otile[:, 2304:2560],
                                                 pmm[:, 256:512]),
               r=[pmm], w=[otile], fsz=256)


def build_combine0(final=False):
    S = Sched()
    x = S.din("x", [PC, D], F32)
    sT_i = S.din("sT", [D, PC], BF16)
    gT_i = S.din("gT", [D, PC], BF16)
    identb = S.din("identb", [128, 128], BF16)
    if final:
        out_o = S.dout("out", [PC, D], F32)
    else:
        x1_o = S.dout("x1", [PC, D], F32)
        xn1_o = S.dout("xn1", [PC, D], BF16)
    sT = S.sb("sTs", [128, 16, PC], BF16)
    gT = S.sb("gTs", [128, 16, PC], BF16)
    pp = S.sb("pp", [128, 16, 128], BF16)
    sgt = [S.sb(f"sgt{i}", [128, D], BF16) for i in range(2)]
    xin = [S.sb(f"xin{i}", [128, D], F32) for i in range(2)]
    x1t = [S.sb(f"x1t{i}", [128, D], F32) for i in range(2)]
    xnt = [S.sb(f"xnt{i}", [128, D], BF16) for i in range(2)]
    tmp = S.sb("tmp", [128, D], BF16)
    sident = S.sb("sident", [128, 128], BF16)
    ptp = S.ps("ptp", [128, 1024], BF16)
    stats = None if final else ln_stats_tiles(S, "st")
    OUT = object()
    dma(S, "sync", sident[:], identb, w=[sident])
    for dc in range(16):
        dma(S, "sync", sT[:, dc, :], sT_i[128 * dc:128 * (dc + 1), :], w=[sT])
        dma(S, "sync", gT[:, dc, :], gT_i[128 * dc:128 * (dc + 1), :], w=[gT])
    for t in range(NTILES):
        b = t % 2
        sl = slice(128 * t, 128 * (t + 1))
        dma(S, "sync", xin[b][:], x[sl, :], w=[xin[b]])
        S.step("vector", lambda v, sl=sl: v.tensor_tensor(
            out=pp[:], in0=sT[:, :, sl], in1=gT[:, :, sl], op=ALU.mult),
            r=[sT, gT], w=[pp], fsz=BIG)
        emit_transpose(S, [(pp[:, j, :], pp) for j in range(16)],
                       [(sgt[b][:, 128 * j:128 * (j + 1)], sgt[b])
                        for j in range(16)], ptp, sident)
        S.step("vector", lambda v, b=b: v.tensor_tensor(
            out=x1t[b][:], in0=xin[b][:], in1=sgt[b][:], op=ALU.add),
            r=[xin[b], sgt[b]], w=[x1t[b]], fsz=BIG)
        if final:
            dma(S, "gpsimd", out_o[sl, :], x1t[b][:], r=[x1t[b]], w=[OUT])
        else:
            emit_ln_rows(S, x1t[b][:], x1t[b], xnt[b][:], xnt[b], tmp, stats)
            dma(S, "gpsimd", x1_o[sl, :], x1t[b][:], r=[x1t[b]], w=[OUT])
            dma(S, "gpsimd", xn1_o[sl, :], xnt[b][:], r=[xnt[b]], w=[OUT])
    return S.finalize()


def build_proj_fm():
    S = Sched()
    aT_i = S.din("aT", [D, PC], BF16)
    w_i = S.din("w", [D, D], F32)
    oT_o = S.dout("oT", [D, PC], BF16)
    aT = S.sb("aTs", [128, 16, PC], BF16)
    wstg = [S.sb(f"wstg{i}", [128, D], F32) for i in range(2)]
    wb = S.sb("wb", [128, 16, D], BF16)
    ot = [S.sb(f"ot{i}", [128, 512], BF16) for i in range(4)]
    pmm = S.ps("pmm", [128, 2048], F32)
    OUT = object()
    for dc in range(16):
        dma(S, "sync", aT[:, dc, :], aT_i[128 * dc:128 * (dc + 1), :], w=[aT])
        b = dc % 2
        dma(S, "sync", wstg[b][:], w_i[128 * dc:128 * (dc + 1), :], w=[wstg[b]])
        S.step("vector", lambda v, dc=dc, b=b: v.tensor_copy(wb[:, dc, :],
                                                             wstg[b][:]),
               r=[wstg[b]], w=[wb], fsz=BIG)
    for oc in range(16):
        for ts in range(4):
            bb = ts % 4
            def mms(tt, oc=oc, ts=ts):
                last = None
                for kc in range(16):
                    last = tt.matmul(pmm[:, 512 * ts:512 * (ts + 1)],
                                     wb[:, kc, 128 * oc:128 * (oc + 1)],
                                     aT[:, kc, 512 * ts:512 * (ts + 1)],
                                     start=(kc == 0), stop=(kc == 15))
                return last
            S.step("tensor", mms, r=[wb, aT], w=[pmm])
            S.step("vector", lambda v, ts=ts, bb=bb: v.tensor_copy(
                ot[bb][:], pmm[:, 512 * ts:512 * (ts + 1)]),
                r=[pmm], w=[ot[bb]], fsz=BIG)
            dma(S, "gpsimd", oT_o[128 * oc:128 * (oc + 1),
                                  512 * ts:512 * (ts + 1)], ot[bb][:],
                r=[ot[bb]], w=[OUT])
    return S.finalize()



def _dequant_emit(S, pk_i, sc_i, w_o, nkc, nspan, span, misc):
    (spk, ssc, snib, st3, sbits, scorr, ss8, ssgn, smag, sval, wout, ssel,
     sshift, sone, psc) = misc
    for kc in range(nkc):
        for sp in range(nspan):
            b = sp % 2
            cs = slice(span * sp, span * (sp + 1))
            dma(S, "sync", spk[b][:], pk_i[128 * kc:128 * (kc + 1), cs],
                w=[spk[b]])
            dma(S, "sync", ssc[b][:], sc_i[4 * kc:4 * (kc + 1), cs],
                w=[ssc[b]])
            def selmm(t, b=b):
                last = None
                for j in range(span // 512):
                    last = t.matmul(psc[:, 512 * j:512 * (j + 1)], ssel[:],
                                    ssc[b][:, 512 * j:512 * (j + 1)],
                                    start=True, stop=True)
                return last
            S.step("tensor", selmm, r=[ssc[b], ssel], w=[psc])
            S.step("vector", lambda v, b=b: v.tensor_scalar(
                out=snib[:], in0=spk[b][:], scalar1=sshift[:], scalar2=15,
                op0=ALU.logical_shift_right, op1=ALU.bitwise_and),
                r=[spk[b], sshift], w=[snib], fsz=BIG)
            S.step("vector", lambda v: v.tensor_scalar(
                out=st3[:], in0=snib[:], scalar1=7, scalar2=22,
                op0=ALU.bitwise_and, op1=ALU.logical_shift_left),
                r=[snib], w=[st3], fsz=BIG)
            S.step("vector", lambda v: v.tensor_scalar(
                out=sbits[:], in0=st3[:], scalar1=1040187392, scalar2=None,
                op0=ALU.bitwise_or), r=[st3], w=[sbits], fsz=BIG)
            S.step("scalar", lambda s: s.activation(
                out=scorr[:], in_=sbits[:].bitcast(F32), func=AF.Relu,
                bias=sone[:], scale=-1.0), r=[sbits, sone], w=[scorr])
            S.step("vector", lambda v: v.tensor_scalar(
                out=ss8[:], in0=snib[:], scalar1=8, scalar2=None,
                op0=ALU.bitwise_and), r=[snib], w=[ss8], fsz=BIG)
            S.step("vector", lambda v: v.tensor_scalar(
                out=ssgn[:], in0=ss8[:], scalar1=-1.0, scalar2=4.0,
                op0=ALU.mult, op1=ALU.add), r=[ss8], w=[ssgn], fsz=BIG)
            S.step("vector", lambda v: v.tensor_tensor(
                out=smag[:], in0=sbits[:].bitcast(F32), in1=scorr[:],
                op=ALU.subtract), r=[sbits, scorr], w=[smag], fsz=BIG)
            S.step("vector", lambda v: v.tensor_tensor(
                out=sval[:], in0=smag[:], in1=ssgn[:], op=ALU.mult),
                r=[smag, ssgn], w=[sval], fsz=BIG)
            S.step("vector", lambda v, b=b: v.tensor_tensor(
                out=wout[b][:], in0=sval[:], in1=psc[:], op=ALU.mult),
                r=[sval, psc], w=[wout[b]], fsz=BIG)
            dma(S, "gpsimd", w_o[128 * kc:128 * (kc + 1), cs], wout[b][:],
                r=[wout[b]], w=[object()])


def build_dequant1():
    S = Sched()
    span = 2048
    gpk = S.din("gpk", [256, 16384], I32)
    upk = S.din("upk", [256, 16384], I32)
    dpk = S.din("dpk", [2048, 2048], I32)
    gsc = S.din("gsc", [8, 16384], F32)
    usc = S.din("usc", [8, 16384], F32)
    dsc = S.din("dsc", [64, 2048], F32)
    shifts = S.din("shifts", [128, 1], I32)
    sel = S.din("sel", [4, 128], F32)
    gw_o = S.dout("gw", [256, 16384], BF16)
    uw_o = S.dout("uw", [256, 16384], BF16)
    dw_o = S.dout("dw", [2048, 2048], BF16)
    spk = [S.sb(f"spk{i}", [128, span], I32) for i in range(2)]
    ssc = [S.sb(f"ssc{i}", [4, span], F32) for i in range(2)]
    snib = S.sb("snib", [128, span], I32)
    st3 = S.sb("st3", [128, span], I32)
    sbits = S.sb("sbits", [128, span], I32)
    scorr = S.sb("scorr", [128, span], F32)
    ss8 = S.sb("ss8", [128, span], I32)
    ssgn = S.sb("ssgn", [128, span], F32)
    smag = S.sb("smag", [128, span], F32)
    sval = S.sb("sval", [128, span], F32)
    wout = [S.sb(f"wout{i}", [128, span], BF16) for i in range(2)]
    ssel = S.sb("ssel", [4, 128], F32)
    sshift = S.sb("sshift", [128, 1], I32)
    sone = S.const_tile("sone", 0.25)
    psc = S.ps("psc", [128, span], F32)
    dma(S, "sync", ssel[:], sel, w=[ssel])
    dma(S, "sync", sshift[:], shifts, w=[sshift])
    misc = (spk, ssc, snib, st3, sbits, scorr, ss8, ssgn, smag, sval, wout,
            ssel, sshift, sone, psc)
    _dequant_emit(S, gpk, gsc, gw_o, 2, 8, span, misc)
    _dequant_emit(S, upk, usc, uw_o, 2, 8, span, misc)
    _dequant_emit(S, dpk, dsc, dw_o, 16, 1, span, misc)
    return S.finalize()



def build_ffn1():
    S = Sched()
    xn1 = S.din("xn1", [PC, D], BF16)
    gw = S.din("gw", [D, 16384], BF16)
    uw = S.din("uw", [D, 16384], BF16)
    dw = S.din("dw", [16384, D], BF16)
    identb = S.din("identb", [128, 128], BF16)
    g1_o = S.dout("g1T", [D, PC], BF16)
    sident = S.sb("sident", [128, 128], BF16)
    xtile = [S.sb(f"xtile{i}", [128, D], BF16) for i in range(2)]
    xq = S.sb("xq", [128, 16, 512], BF16)
    acc = S.sb("acc", [128, 16, 512], F32)
    gbuf = [S.sb(f"gbuf{i}", [128, 16, 512], BF16) for i in range(2)]
    ubuf = [S.sb(f"ubuf{i}", [128, 16, 512], BF16) for i in range(2)]
    dbuf = [S.sb(f"dbuf{i}", [128, 4, D], BF16) for i in range(2)]
    ibuf = S.sb("ibuf", [128, 8, 512], BF16)
    tgel = S.sb("tgel", [128, 512], F32)
    gout = [S.sb(f"gout{i}", [128, 512], BF16) for i in range(2)]
    ptp = S.ps("ptp", [128, 1024], BF16)
    pG = S.ps("pG", [128, 512], F32)
    pU = S.ps("pU", [128, 512], F32)
    pD = S.ps("pD", [128, 512], F32)
    OUT = object()
    dma(S, "sync", sident[:], identb, w=[sident])
    for qtr in range(4):
        q0 = 512 * qtr
        for tt in range(4):
            b = tt % 2
            dma(S, "sync", xtile[b][:],
                xn1[q0 + 128 * tt:q0 + 128 * (tt + 1), :], w=[xtile[b]])
            emit_transpose(S, [(xtile[b][:, 128 * j:128 * (j + 1)], xtile[b])
                               for j in range(16)],
                           [(xq[:, j, 128 * tt:128 * (tt + 1)], xq)
                            for j in range(16)], ptp, sident)
        S.step("vector", lambda v: v.memset(acc[:], 0.0), w=[acc], fsz=BIG)
        for sl8 in range(16):
            for half in range(2):
                st = half
                n0 = 1024 * sl8 + 512 * half
                for kc in range(16):
                    dma(S, "scalar", gbuf[st][:, kc, :],
                        gw[128 * kc:128 * (kc + 1), n0:n0 + 512], w=[gbuf[st]])
                    dma(S, "scalar", ubuf[st][:, kc, :],
                        uw[128 * kc:128 * (kc + 1), n0:n0 + 512], w=[ubuf[st]])
                for j in range(4):
                    ng = n0 + 128 * j
                    dma(S, "sync", dbuf[st][:, j, :], dw[ng:ng + 128, :],
                        w=[dbuf[st]])
                for j in range(4):
                    def gmm(t, st=st, j=j):
                        last = None
                        for kc in range(16):
                            last = t.matmul(pG[:], gbuf[st][:, kc,
                                                           128 * j:128 * (j + 1)],
                                            xq[:, kc, :], start=(kc == 0),
                                            stop=(kc == 15))
                        return last
                    S.step("tensor", gmm, r=[gbuf[st], xq], w=[pG])
                    def umm(t, st=st, j=j):
                        last = None
                        for kc in range(16):
                            last = t.matmul(pU[:], ubuf[st][:, kc,
                                                           128 * j:128 * (j + 1)],
                                            xq[:, kc, :], start=(kc == 0),
                                            stop=(kc == 15))
                        return last
                    S.step("tensor", umm, r=[ubuf[st], xq], w=[pU])
                    S.step("scalar", lambda s: s.activation(
                        out=tgel[:], in_=pG[:], func=AF.Gelu_apprx_tanh),
                        r=[pG], w=[tgel], fsz=BIG)
                    S.step("vector", lambda v, half=half, j=j: v.tensor_tensor(
                        out=ibuf[:, 4 * half + j, :], in0=tgel[:], in1=pU[:],
                        op=ALU.mult), r=[tgel, pU], w=[ibuf], fsz=BIG)
            for hc in range(16):
                def dmm(t, hc=hc):
                    last = None
                    for nci in range(8):
                        last = t.matmul(pD[:],
                                        dbuf[nci // 4][:, nci % 4,
                                                       128 * hc:128 * (hc + 1)],
                                        ibuf[:, nci, :], start=(nci == 0),
                                        stop=(nci == 7))
                    return last
                S.step("tensor", dmm, r=[dbuf[0], dbuf[1], ibuf], w=[pD])
                S.step("vector", lambda v, hc=hc: v.tensor_tensor(
                    out=acc[:, hc, :], in0=acc[:, hc, :], in1=pD[:],
                    op=ALU.add), r=[acc, pD], w=[acc], fsz=BIG)
        for hc in range(16):
            b = hc % 2
            S.step("scalar", lambda s, hc=hc, b=b: s.activation(
                out=gout[b][:], in_=acc[:, hc, :], func=AF.Silu),
                r=[acc], w=[gout[b]], fsz=BIG)
            dma(S, "gpsimd", g1_o[128 * hc:128 * (hc + 1), q0:q0 + 512],
                gout[b][:], r=[gout[b]], w=[OUT])
    return S.finalize()



def build_moe0():
    S = Sched()
    xn = S.din("xn", [PC, D], BF16)
    x_i = S.din("x", [PC, D], F32)
    gwT_i = S.din("gwT", [D, 8], F32)
    egT = S.din("egT", [8, D, 1408], F32)
    euT = S.din("euT", [8, D, 1408], F32)
    edT = S.din("edT", [8, 1408, D], F32)
    sgT = S.din("sgT", [D, 2816], F32)
    suT = S.din("suT", [D, 2816], F32)
    sdT = S.din("sdT", [2816, D], F32)
    identb = S.din("identb", [128, 128], BF16)
    identf = S.din("identf", [128, 128], F32)
    sel8_i = S.din("sel8", [8, 1024], F32)
    gT_o = S.dout("gT", [D, PC], BF16)

    sident = S.sb("sident", [128, 128], BF16)
    sidentf = S.sb("sidentf", [128, 128], F32)
    xtile = [S.sb(f"xtile{i}", [128, D], BF16) for i in range(2)]
    xnT = S.sb("xnT", [128, 16, PC], BF16)
    gwstg = S.sb("gwstg", [128, 16, 8], F32)
    gstats = None
    gt = {k: S.sb(f"gt_{k}", [128, 8], F32)
          for k in ("probs", "mask1", "p2", "mask2", "msum", "sel", "c")}
    gs = {k: S.sb(f"gs_{k}", [128, 1], F32)
          for k in ("nmx", "sumex", "rs", "m1", "m2", "den", "rden")}
    call_c = S.sb("call_c", [128, 16, 8], F32)
    crt = S.sb("crt", [8, 16, 128], F32)
    acc = S.sb("acc", [128, 16, 1024], F32)
    wga = [S.sb(f"wga{i}", [128, 16, 512], BF16) for i in range(2)]
    sdwb0 = S.sb("sdwb0", [128, 2, D], BF16)
    sdwb = [sdwb0, sdwb0]
    hq = S.sb("hq", [128, 4, 1024], BF16)
    tf = S.sb("tf", [128, 1024], F32)
    tfu = S.sb("tfu", [128, 1024], F32)
    ssel8 = S.sb("ssel8", [8, 1024], F32)
    goutt = [xtile[0], xtile[1]]
    pA = S.ps("pA", [128, 1024], F32)
    pB = S.ps("pB", [128, 1024], F32)
    pC = S.ps("pC", [128, 1024], F32)
    pD = S.ps("pD", [128, 1024], F32)
    ptpv = pA[:].bitcast(BF16)
    OUT = object()

    dma(S, "sync", sident[:], identb, w=[sident])
    dma(S, "sync", sidentf[:], identf, w=[sidentf])
    dma(S, "sync", gwstg[:], gwT_i.rearrange("(kc p) e -> p kc e", p=128),
        w=[gwstg])
    dma(S, "sync", ssel8[:], sel8_i, w=[ssel8])
    gstats = ln_stats_tiles(S, "gst")
    ghi = S.sb("ghi", [128, 16, 8], BF16)
    glo = S.sb("glo", [128, 16, 8], BF16)
    glof = S.sb("glof", [128, 16, 8], F32)
    accflat = acc[:].rearrange("p a b -> p (a b)")
    xin32 = accflat[:, 0:2048]
    xn32 = accflat[:, 2048:4096]
    accbf = acc[:].rearrange("p a b -> p (a b)").bitcast(BF16)
    xhi = accbf[:, 8192:10240]
    xlo = accbf[:, 10240:12288]
    gtmp = accbf[:, 12288:14336]
    hiT = accbf[:, 14336:16384].rearrange("p (a b) -> p a b", b=128)
    loT = accbf[:, 16384:18432].rearrange("p (a b) -> p a b", b=128)
    S.step("vector", lambda v: v.tensor_copy(ghi[:], gwstg[:]), r=[gwstg],
           w=[ghi])
    S.step("vector", lambda v: v.tensor_tensor(out=glof[:], in0=gwstg[:],
                                               in1=ghi[:], op=ALU.subtract),
           r=[gwstg, ghi], w=[glof])
    S.step("vector", lambda v: v.tensor_copy(glo[:], glof[:]), r=[glof],
           w=[glo])
    for tt in range(16):
        b = tt % 2
        dma(S, "sync", xtile[b][:], xn[128 * tt:128 * (tt + 1), :],
            w=[xtile[b]])
        emit_transpose(S, [(xtile[b][:, 128 * j:128 * (j + 1)], xtile[b])
                           for j in range(16)],
                       [(xnT[:, j, 128 * tt:128 * (tt + 1)], xnT)
                        for j in range(16)], ptp=ptpv, ident=sident,
                       ptp_owner=pA)
        dma(S, "sync", xin32, x_i[128 * tt:128 * (tt + 1), :], w=[acc])
        emit_ln_rows_v(S, xin32, acc, xn32, acc, gtmp, acc, gstats)
        S.step("vector", lambda v: v.tensor_copy(xhi, xn32), r=[acc],
               w=[acc], fsz=BIG)
        S.step("vector", lambda v: v.scalar_tensor_tensor(
            out=xlo, in0=xhi, scalar=-1.0, in1=xn32, op0=ALU.mult,
            op1=ALU.add), r=[acc], w=[acc], fsz=BIG)
        emit_transpose(S, [(xhi.rearrange("p (a b) -> p a b", b=128)[:, j, :],
                            acc) for j in range(16)],
                       [(hiT[:, j, :], acc) for j in range(16)],
                       ptpv, sident, ptp_owner=pA)
        emit_transpose(S, [(xlo.rearrange("p (a b) -> p a b", b=128)[:, j, :],
                            acc) for j in range(16)],
                       [(loT[:, j, :], acc) for j in range(16)],
                       ptpv, sident, ptp_owner=pA)
        def logmm(t):
            last = None
            for i, (xt, gw_) in enumerate(((hiT, ghi), (hiT, glo),
                                           (loT, ghi))):
                for kc in range(16):
                    last = t.matmul(pB[:, 0:8], xt[:, kc, :], gw_[:, kc, :],
                                    start=(i == 0 and kc == 0),
                                    stop=(i == 2 and kc == 15))
            return last
        S.step("tensor", logmm, r=[acc, ghi, glo], w=[pB])
        S.step("vector", lambda v: v.reduce_max(gs["nmx"][:], pB[:, 0:8],
                                                axis=AXX, negate=True),
               r=[pB], w=[gs["nmx"]])
        S.step("scalar", lambda s: s.activation(
            out=gt["probs"][:], in_=pB[:, 0:8], func=AF.Exp,
            bias=gs["nmx"][:], accum_out=gs["sumex"][:]),
            r=[pB, gs["nmx"]], w=[gt["probs"], gs["sumex"]])
        S.step("vector", lambda v: v.reciprocal(gs["rs"][:], gs["sumex"][:]),
               r=[gs["sumex"]], w=[gs["rs"]])
        S.step("vector", lambda v: v.tensor_scalar(
            out=gt["probs"][:], in0=gt["probs"][:], scalar1=gs["rs"][:],
            scalar2=None, op0=ALU.mult), r=[gt["probs"], gs["rs"]],
            w=[gt["probs"]])
        S.step("vector", lambda v: v.reduce_max(gs["m1"][:], gt["probs"][:],
                                                axis=AXX),
               r=[gt["probs"]], w=[gs["m1"]])
        S.step("vector", lambda v: v.tensor_scalar(
            out=gt["mask1"][:], in0=gt["probs"][:], scalar1=gs["m1"][:],
            scalar2=None, op0=ALU.is_ge), r=[gt["probs"], gs["m1"]],
            w=[gt["mask1"]])
        S.step("vector", lambda v: v.scalar_tensor_tensor(
            out=gt["p2"][:], in0=gt["mask1"][:], scalar=-4.0,
            in1=gt["probs"][:], op0=ALU.mult, op1=ALU.add),
            r=[gt["mask1"], gt["probs"]], w=[gt["p2"]])
        S.step("vector", lambda v: v.reduce_max(gs["m2"][:], gt["p2"][:],
                                                axis=AXX),
               r=[gt["p2"]], w=[gs["m2"]])
        S.step("vector", lambda v: v.tensor_scalar(
            out=gt["mask2"][:], in0=gt["p2"][:], scalar1=gs["m2"][:],
            scalar2=None, op0=ALU.is_ge), r=[gt["p2"], gs["m2"]],
            w=[gt["mask2"]])
        S.step("vector", lambda v: v.tensor_tensor(
            out=gs["den"][:], in0=gs["m1"][:], in1=gs["m2"][:], op=ALU.add),
            r=[gs["m1"], gs["m2"]], w=[gs["den"]])
        S.step("vector", lambda v: v.tensor_scalar(
            out=gs["den"][:], in0=gs["den"][:], scalar1=1e-20, scalar2=None,
            op0=ALU.add), r=[gs["den"]], w=[gs["den"]])
        S.step("vector", lambda v: v.reciprocal(gs["rden"][:], gs["den"][:]),
               r=[gs["den"]], w=[gs["rden"]])
        S.step("vector", lambda v: v.tensor_tensor(
            out=gt["msum"][:], in0=gt["mask1"][:], in1=gt["mask2"][:],
            op=ALU.add), r=[gt["mask1"], gt["mask2"]], w=[gt["msum"]])
        S.step("vector", lambda v: v.tensor_tensor(
            out=gt["sel"][:], in0=gt["msum"][:], in1=gt["probs"][:],
            op=ALU.mult), r=[gt["msum"], gt["probs"]], w=[gt["sel"]])
        S.step("vector", lambda v, tt=tt: v.tensor_scalar(
            out=call_c[:, tt, :], in0=gt["sel"][:], scalar1=gs["rden"][:],
            scalar2=None, op0=ALU.mult), r=[gt["sel"], gs["rden"]],
            w=[call_c])
        S.step("tensor", lambda t, tt=tt: t.transpose(
            pD[0:8, 0:128], call_c[:, tt, :], sidentf[:]),
            r=[call_c, sidentf], w=[pD])
        S.step("vector", lambda v, tt=tt: v.tensor_copy(crt[:, tt, :],
                                                        pD[0:8, 0:128]),
            r=[pD], w=[crt])

    def ffn_block(grp, gT_w, uT_w, dT_w, n_ic, pieces, cb_psum):
        t0 = 1024 * grp
        ic0 = 0
        for pc in pieces:
            for icl in range(pc):
                ic = ic0 + icl
                st = icl % 2
                for kc in range(16):
                    dma(S, "gpsimd", wga[st][:, kc, 0:128],
                        gT_w[128 * kc:128 * (kc + 1), 128 * ic:128 * (ic + 1)],
                        w=[wga[st]])
                    dma(S, "gpsimd", wga[st][:, kc, 128:256],
                        uT_w[128 * kc:128 * (kc + 1), 128 * ic:128 * (ic + 1)],
                        w=[wga[st]])
                def gmm(t, st=st, t0=t0):
                    last = None
                    for half in range(2):
                        for kc in range(16):
                            last = t.matmul(
                                pA[:, 512 * half:512 * (half + 1)],
                                wga[st][:, kc, 0:128],
                                xnT[:, kc, t0 + 512 * half:t0 + 512 * (half + 1)],
                                start=(kc == 0), stop=(kc == 15))
                    return last
                S.step("tensor", gmm, r=[wga[st], xnT], w=[pA])
                S.step("scalar", lambda s: s.activation(
                    out=tf[:], in_=pA[:], func=AF.Silu), r=[pA], w=[tf],
                    fsz=BIG)
                def ummi(t, st=st, t0=t0):
                    last = None
                    for half in range(2):
                        for kc in range(16):
                            last = t.matmul(
                                pB[:, 512 * half:512 * (half + 1)],
                                wga[st][:, kc, 128:256],
                                xnT[:, kc, t0 + 512 * half:t0 + 512 * (half + 1)],
                                start=(kc == 0), stop=(kc == 15))
                    return last
                S.step("tensor", ummi, r=[wga[st], xnT], w=[pB])
                if cb_psum is None:
                    S.step("vector", lambda v, icl=icl: v.tensor_tensor(
                        out=hq[:, icl, :], in0=tf[:], in1=pB[:], op=ALU.mult),
                        r=[tf, pB], w=[hq], fsz=BIG)
                else:
                    S.step("vector", lambda v: v.tensor_tensor(
                        out=tfu[:], in0=tf[:], in1=pB[:], op=ALU.mult),
                        r=[tf, pB], w=[tfu], fsz=BIG)
                    S.step("vector", lambda v, icl=icl: v.tensor_tensor(
                        out=hq[:, icl, :], in0=tfu[:], in1=cb_psum,
                        op=ALU.mult), r=[tfu, pC], w=[hq], fsz=BIG)
            for ps in range((pc + 1) // 2):
                ids = [i for i in (2 * ps, 2 * ps + 1) if i < pc]
                st = ps % 2
                for i, icl in enumerate(ids):
                    ic = ic0 + icl
                    dma(S, "gpsimd", sdwb[st][:, i, :],
                        dT_w[128 * ic:128 * (ic + 1), :], w=[sdwb[st]])
                for hc in range(16):
                    def dmm(t, st=st, ids=ids, hc=hc):
                        last = None
                        for half in range(2):
                            for i, icl in enumerate(ids):
                                last = t.matmul(
                                    pD[:, 512 * half:512 * (half + 1)],
                                    sdwb[st][:, i, 128 * hc:128 * (hc + 1)],
                                    hq[:, icl, 512 * half:512 * (half + 1)],
                                    start=(i == 0), stop=(i == len(ids) - 1))
                        return last
                    S.step("tensor", dmm, r=[sdwb[st], hq], w=[pD])
                    S.step("vector", lambda v, hc=hc: v.tensor_tensor(
                        out=acc[:, hc, :], in0=acc[:, hc, :], in1=pD[:],
                        op=ALU.add), r=[acc, pD], w=[acc], fsz=BIG)
            ic0 += pc

    for grp in range(2):
        S.step("vector", lambda v: v.memset(acc[:], 0.0), w=[acc], fsz=BIG)
        ffn_block(grp, sgT, suT, sdT, 22, [4, 4, 4, 4, 4, 2], None)
        for e in range(8):
            def cbmm(t, e=e, grp=grp):
                last = None
                crtf = crt[:].rearrange("e t q -> e (t q)")
                for half in range(2):
                    last = t.matmul(pC[:, 512 * half:512 * (half + 1)],
                                    ssel8[:, 128 * e:128 * (e + 1)],
                                    crtf[:, 1024 * grp + 512 * half:
                                         1024 * grp + 512 * (half + 1)],
                                    start=True, stop=True)
                return last
            S.step("tensor", cbmm, r=[ssel8, crt], w=[pC])
            ffn_block(grp, egT[e], euT[e], edT[e], 11, [4, 4, 3], pC[:])
        for hc in range(16):
            b = hc % 2
            S.step("scalar", lambda s, hc=hc, b=b: s.activation(
                out=goutt[b][:, 0:1024], in_=acc[:, hc, :], func=AF.Silu),
                r=[acc], w=[goutt[b]], fsz=BIG)
            dma(S, "gpsimd", gT_o[128 * hc:128 * (hc + 1),
                                  1024 * grp:1024 * (grp + 1)],
                goutt[b][:, 0:1024], r=[goutt[b]], w=[OUT])
    return S.finalize()


SCALE0 = 0.08838834764831845

def build_attn0():
    S = Sched()
    q_i = S.din("q", [PC, D], BF16)
    k_i = S.din("k", [SEQ, 256], BF16)
    v_i = S.din("v", [SEQ, 256], BF16)
    identb = S.din("identb", [128, 128], BF16)
    identf = S.din("identf", [128, 128], F32)
    qpos_i = S.din("qpos", [128, 16], F32)
    iotak_i = S.din("iotak", [128, SEQ], F32)
    sel16_i = S.din("sel16", [16, 2048], F32)
    sp_o = S.dout("spre", [D, PC], BF16)

    sident = S.sb("sident", [128, 128], BF16)
    sidentf = S.sb("sidentf", [128, 128], F32)
    sqpos = S.sb("sqpos", [128, 16], F32)
    siota = S.sb("siota", [128, SEQ], F32)
    ssel16 = S.sb("ssel16", [16, 2048], F32)
    ktile = [S.sb(f"ktile{i}", [128, 256], BF16) for i in range(2)]
    kT = S.sb("kT", [128, 2, SEQ], BF16)
    sv = S.sb("sv", [128, 32, 256], BF16)
    qtile = [S.sb(f"qtile{i}", [128, D], BF16) for i in range(2)]
    qT = S.sb("qT", [128, 16, 128], BF16)
    mall = S.sb("mall", [128, 8, 512], F32)
    srow = S.sb("srow", [128, 2, SEQ], F32)
    pgrp = S.sb("pgrp", [128, 8, SEQ], BF16)
    pTt = [S.sb(f"pTt{i}", [128, 8, 128], BF16) for i in range(2)]
    rsacc = S.sb("rsacc", [128, 16], F32)
    rr = S.sb("rr", [128, 16], F32)
    rrT = S.sb("rrT", [16, 128], F32)
    pNs = S.sb("pNs", [128, 128], F32)
    sout = [S.sb(f"sout{i}", [128, 128], BF16) for i in range(2)]
    pS = S.ps("pS", [128, 2, 512], F32)
    ptp = S.ps("ptp", [128, 1024], BF16)
    pO = S.ps("pO", [128, 2048], F32)
    pN = S.ps("pN", [128, 128], F32)
    OUT = object()

    dma(S, "sync", sident[:], identb, w=[sident])
    dma(S, "sync", sidentf[:], identf, w=[sidentf])
    dma(S, "sync", sqpos[:], qpos_i, w=[sqpos])
    dma(S, "sync", siota[:], iotak_i, w=[siota])
    dma(S, "sync", ssel16[:], sel16_i, w=[ssel16])
    dma(S, "sync", sv[:], v_i.rearrange("(kt p) dv -> p kt dv", p=128),
        w=[sv])
    for kt in range(32):
        b = kt % 2
        dma(S, "sync", ktile[b][:], k_i[128 * kt:128 * (kt + 1), :],
            w=[ktile[b]])
        emit_transpose(S, [(ktile[b][:, 128 * j:128 * (j + 1)], ktile[b])
                           for j in range(2)],
                       [(kT[:, j, 128 * kt:128 * (kt + 1)], kT)
                        for j in range(2)], ptp, sident)
    for t in range(16):
        b = t % 2
        dma(S, "sync", qtile[b][:], q_i[128 * t:128 * (t + 1), :],
            w=[qtile[b]])
        emit_transpose(S, [(qtile[b][:, 128 * j:128 * (j + 1)], qtile[b])
                           for j in range(16)],
                       [(qT[:, j, :], qT) for j in range(16)], ptp, sident)
        for c in range(8):
            S.step("vector", lambda v, c=c, t=t: v.tensor_scalar(
                out=mall[:, c, :], in0=siota[:, 512 * c:512 * (c + 1)],
                scalar1=sqpos[:, t:t + 1], scalar2=None, op0=ALU.is_gt),
                r=[siota, sqpos], w=[mall], fsz=BIG)
        for kvg in range(2):
            for pr in range(4):
                h0 = 8 * kvg + 2 * pr
                for c in range(8):
                    def qk(tt, h0=h0, kvg=kvg, c=c):
                        last = None
                        for i in range(2):
                            last = tt.matmul(pS[:, i, :], qT[:, h0 + i, :],
                                             kT[:, kvg,
                                                512 * c:512 * (c + 1)],
                                             start=True, stop=True)
                        return last
                    S.step("tensor", qk, r=[qT, kT], w=[pS])
                    S.step("scalar", lambda s, c=c: s.activation(
                        out=srow[:, :, 512 * c:512 * (c + 1)], in_=pS[:],
                        func=AF.Tanh, scale=SCALE0 / 30.0),
                        r=[pS], w=[srow], fsz=BIG)
                S.step("vector", lambda v: v.scalar_tensor_tensor(
                    out=srow[:], in0=mall[:].rearrange("p a b -> p (a b)")
                    .unsqueeze(1).broadcast_to([128, 2, SEQ]),
                    scalar=-40.0, in1=srow[:], op0=ALU.mult, op1=ALU.add),
                    r=[mall, srow], w=[srow], fsz=BIG)
                for i in range(2):
                    S.step("scalar", lambda s, i=i, h0=h0, pr=pr:
                           s.activation(
                               out=pgrp[:, 2 * pr + i, :],
                               in_=srow[:, i, :], func=AF.Exp, scale=30.0,
                               accum_out=rsacc[:, h0 + i:h0 + i + 1]),
                           r=[srow], w=[pgrp, rsacc], fsz=BIG)
            for kt in range(32):
                b2 = kt % 2
                emit_transpose(S, [(pgrp[:, j, 128 * kt:128 * (kt + 1)], pgrp)
                                   for j in range(8)],
                               [(pTt[b2][:, j, :], pTt[b2])
                                for j in range(8)], ptp, sident)
                def pv(tt, kt=kt, kvg=kvg, b2=b2):
                    rhs = pTt[b2][:].rearrange("p h q -> p (h q)")
                    last = None
                    for half in range(2):
                        last = tt.matmul(
                            pO[:, 1024 * kvg + 512 * half:
                               1024 * kvg + 512 * (half + 1)],
                            sv[:, kt, 128 * kvg:128 * (kvg + 1)],
                            rhs[:, 512 * half:512 * (half + 1)],
                            start=(kt == 0), stop=(kt == 31))
                    return last
                S.step("tensor", pv, r=[sv, pTt[b2]], w=[pO])
        S.step("vector", lambda v: v.reciprocal(rr[:], rsacc[:]),
               r=[rsacc], w=[rr])
        S.step("tensor", lambda tt: tt.transpose(pN[0:16, 0:128], rr[:],
                                                 sidentf[:]),
               r=[rr, sidentf], w=[pN])
        S.step("vector", lambda v: v.tensor_copy(rrT[:], pN[0:16, 0:128]),
               r=[pN], w=[rrT])
        for h in range(16):
            b3 = h % 2
            S.step("tensor", lambda tt, h=h: tt.matmul(
                pN[:], ssel16[:, 128 * h:128 * (h + 1)], rrT[:],
                start=True, stop=True), r=[ssel16, rrT], w=[pN])
            S.step("vector", lambda v: v.tensor_copy(pNs[:], pN[:]),
                   r=[pN], w=[pNs], fsz=128)
            S.step("vector", lambda v, h=h, b3=b3: v.tensor_tensor(
                out=sout[b3][:], in0=pO[:, 128 * h:128 * (h + 1)],
                in1=pNs[:], op=ALU.mult), r=[pO, pNs], w=[sout[b3]],
                fsz=128)
            dma(S, "gpsimd", sp_o[128 * h:128 * (h + 1),
                                  128 * t:128 * (t + 1)], sout[b3][:],
                r=[sout[b3]], w=[OUT])
    return S.finalize()



def build_attn1():
    S = Sched()
    LT = 17
    xh_i = S.din("xh", [2176, D], BF16)
    wq_i = S.din("wq", [D, D], F32)
    wkv_i = S.din("wkv", [D, 1024], F32)
    sinkb_i = S.din("sinkb", [128, 16], F32)
    cosq_i = S.din("cosq", [PC, 64], F32)
    sinq_i = S.din("sinq", [PC, 64], F32)
    cosk_i = S.din("cosk", [2176, 64], F32)
    sink_i = S.din("sinkt", [2176, 64], F32)
    combm_i = S.din("combm", [16, 128, 256], F32)
    identb = S.din("identb", [128, 128], BF16)
    identf = S.din("identf", [128, 128], F32)
    sel16_i = S.din("sel16", [16, 2048], F32)
    sp_o = S.dout("spre", [D, PC], BF16)

    sident = S.sb("sident", [128, 128], BF16)
    sidentf = S.sb("sidentf", [128, 128], F32)
    ssel16 = S.sb("ssel16", [16, 2048], F32)
    ssink = S.sb("ssink", [128, 16], F32)
    scosq = S.sb("scosq", [128, 16, 64], F32)
    ssinq = S.sb("ssinq", [128, 16, 64], F32)
    scosk = S.sb("scosk", [128, LT, 64], F32)
    ssink_t = S.sb("ssink_t", [128, LT, 64], F32)
    scomb = S.sb("scomb", [128, 16, 256], F32)
    wstg = [S.sb(f"wstg{i}", [128, 1024], F32) for i in range(2)]
    wqb = S.sb("wqb", [128, 16, D], BF16)
    wkvb = wqb[:].rearrange("p a b -> p (a b)")[:, 0:16384].rearrange(
        "p (a b) -> p a b", b=1024)
    xtile = [S.sb(f"xtile{i}", [128, D], BF16) for i in range(2)]
    xnT = [S.sb(f"xnT{i}", [128, 16, 128], BF16) for i in range(2)]
    kvrot = S.sb("kvrot", [128, 512], BF16)
    kT = S.sb("kT", [128, 4, 2176], BF16)
    sv = S.sb("sv", [128, LT, 512], BF16)
    qrt = S.sb("qrt", [128, D], BF16)
    qT = S.sb("qT", [128, 16, 128], BF16)
    ta = S.sb("ta", [128, 1024], F32)
    tb = S.sb("tb", [128, 1024], F32)
    srow1 = S.sb("srow1", [128, 2, 256], F32)
    nmx = S.sb("nmx", [128, 2], F32)
    pgrp1 = S.sb("pgrp1", [128, 16, 256], BF16)
    pT1 = [S.sb(f"pT1{i}", [128, 16, 128], BF16) for i in range(2)]
    rsacc = S.sb("rsacc", [128, 16], F32)
    rr = S.sb("rr", [128, 16], F32)
    rrT = S.sb("rrT", [16, 128], F32)
    pNs = S.sb("pNs", [128, 128], F32)
    sout = [S.sb(f"sout{i}", [128, 128], BF16) for i in range(2)]
    pA = S.ps("pA", [128, 2048], F32)
    pS = S.ps("pS", [128, 512], F32)
    ptp = S.ps("ptp", [128, 1024], BF16)
    pN = S.ps("pN", [128, 128], F32)
    OUT = object()

    for nm, ap in ((sident, identb), (sidentf, identf), (ssel16, sel16_i),
                   (ssink, sinkb_i)):
        dma(S, "sync", nm[:], ap, w=[nm])
    dma(S, "sync", scosq[:], cosq_i.rearrange("(t p) f -> p t f", p=128),
        w=[scosq])
    dma(S, "sync", ssinq[:], sinq_i.rearrange("(t p) f -> p t f", p=128),
        w=[ssinq])
    dma(S, "sync", scosk[:], cosk_i.rearrange("(t p) f -> p t f", p=128),
        w=[scosk])
    dma(S, "sync", ssink_t[:], sink_i.rearrange("(t p) f -> p t f", p=128),
        w=[ssink_t])
    dma(S, "sync", scomb[:], combm_i.rearrange("t p k -> p t k"), w=[scomb])
    for kc in range(16):
        b = kc % 2
        dma(S, "sync", wstg[b][:], wkv_i[128 * kc:128 * (kc + 1), :],
            w=[wstg[b]])
        S.step("vector", lambda v, kc=kc, b=b: v.tensor_copy(
            wkvb[:, kc, :], wstg[b][:]), r=[wstg[b]], w=[wqb], fsz=BIG)
    for lt in range(LT):
        b = lt % 2
        dma(S, "sync", xtile[b][:], xh_i[128 * lt:128 * (lt + 1), :],
            w=[xtile[b]])
        emit_transpose(S, [(xtile[b][:, 128 * j:128 * (j + 1)], xtile[b])
                           for j in range(16)],
                       [(xnT[b][:, j, :], xnT[b]) for j in range(16)],
                       ptp, sident)
        def kvmm(t, b=b):
            last = None
            for half in range(2):
                for kc in range(16):
                    last = t.matmul(pA[:, 512 * half:512 * (half + 1)],
                                    xnT[b][:, kc, :],
                                    wkvb[:, kc, 512 * half:512 * (half + 1)],
                                    start=(kc == 0), stop=(kc == 15))
            return last
        S.step("tensor", kvmm, r=[xnT[b], wqb], w=[pA])
        emit_rope_tm(S, pA, 0, 4, kvrot, 0, scosk, ssink_t, lt, ta, tb)
        S.step("vector", lambda v, lt=lt: v.tensor_copy(
            sv[:, lt, :], pA[:, 512:1024]), r=[pA], w=[sv], fsz=BIG)
        emit_transpose(S, [(kvrot[:, 128 * j:128 * (j + 1)], kvrot)
                           for j in range(4)],
                       [(kT[:, j, 128 * lt:128 * (lt + 1)], kT)
                        for j in range(4)], ptp, sident)
    for kc in range(16):
        b = kc % 2
        for half in range(2):
            dma(S, "scalar", wstg[b][:], wq_i[128 * kc:128 * (kc + 1),
                                              1024 * half:1024 * (half + 1)],
                w=[wstg[b]])
            S.step("vector", lambda v, kc=kc, b=b, half=half: v.tensor_copy(
                wqb[:, kc, 1024 * half:1024 * (half + 1)], wstg[b][:]),
                r=[wstg[b]], w=[wqb], fsz=BIG)
    for t in range(16):
        b = t % 2
        dma(S, "sync", xtile[b][:], xh_i[64 + 128 * t:192 + 128 * t, :],
            w=[xtile[b]])
        emit_transpose(S, [(xtile[b][:, 128 * j:128 * (j + 1)], xtile[b])
                           for j in range(16)],
                       [(xnT[b][:, j, :], xnT[b]) for j in range(16)],
                       ptp, sident)
        def qmm(t_, b=b):
            last = None
            for oc in range(4):
                for kc in range(16):
                    last = t_.matmul(pA[:, 512 * oc:512 * (oc + 1)],
                                     xnT[b][:, kc, :],
                                     wqb[:, kc, 512 * oc:512 * (oc + 1)],
                                     start=(kc == 0), stop=(kc == 15))
            return last
        S.step("tensor", qmm, r=[xnT[b], wqb], w=[pA])
        emit_rope_tm(S, pA, 0, 16, qrt, 0, scosq, ssinq, t, ta, tb)
        emit_transpose(S, [(qrt[:, 128 * j:128 * (j + 1)], qrt)
                           for j in range(16)],
                       [(qT[:, j, :], qT) for j in range(16)], ptp, sident)
        for hp in range(8):
            for i in range(2):
                h = 2 * hp + i
                S.step("tensor", lambda t_, h=h, i=i, t=t: t_.matmul(
                    pS[:, 256 * i:256 * (i + 1)], qT[:, h, :],
                    kT[:, h // 4, 128 * t:128 * t + 256], start=True,
                    stop=True), r=[qT, kT], w=[pS])
                S.step("scalar", lambda s, h=h, i=i: s.activation(
                    out=srow1[:, i, :], in_=pS[:, 256 * i:256 * (i + 1)],
                    func=AF.Identity, scale=SCALE0, bias=ssink[:, h:h + 1]),
                    r=[pS, ssink], w=[srow1], fsz=256)
            S.step("vector", lambda v, t=t: v.tensor_tensor(
                out=srow1[:], in0=srow1[:],
                in1=scomb[:, t, :].unsqueeze(1).broadcast_to([128, 2, 256]),
                op=ALU.add), r=[srow1, scomb], w=[srow1], fsz=BIG)
            for i in range(2):
                h = 2 * hp + i
                S.step("vector", lambda v, i=i: v.reduce_max(
                    nmx[:, i:i + 1], srow1[:, i, :], axis=AXX, negate=True),
                    r=[srow1], w=[nmx])
                S.step("scalar", lambda s, h=h, i=i: s.activation(
                    out=pgrp1[:, h, :], in_=srow1[:, i, :], func=AF.Exp,
                    bias=nmx[:, i:i + 1],
                    accum_out=rsacc[:, h:h + 1]),
                    r=[srow1, nmx], w=[pgrp1, rsacc], fsz=256)
        for kt2 in range(2):
            emit_transpose(S, [(pgrp1[:, j, 128 * kt2:128 * (kt2 + 1)], pgrp1)
                               for j in range(16)],
                           [(pT1[kt2][:, j, :], pT1[kt2])
                            for j in range(16)], ptp, sident)
            def pv(t_, kt2=kt2, t=t):
                rhs = pT1[kt2][:].rearrange("p h q -> p (h q)")
                last = None
                for kv in range(4):
                    last = t_.matmul(pA[:, 512 * kv:512 * (kv + 1)],
                                     sv[:, t + kt2, 128 * kv:128 * (kv + 1)],
                                     rhs[:, 512 * kv:512 * (kv + 1)],
                                     start=(kt2 == 0), stop=(kt2 == 1))
                return last
            S.step("tensor", pv, r=[sv, pT1[kt2]], w=[pA])
        S.step("vector", lambda v: v.reciprocal(rr[:], rsacc[:]),
               r=[rsacc], w=[rr])
        S.step("tensor", lambda t_: t_.transpose(pN[0:16, 0:128], rr[:],
                                                 sidentf[:]),
               r=[rr, sidentf], w=[pN])
        S.step("vector", lambda v: v.tensor_copy(rrT[:], pN[0:16, 0:128]),
               r=[pN], w=[rrT])
        for h in range(16):
            b3 = h % 2
            S.step("tensor", lambda t_, h=h: t_.matmul(
                pN[:], ssel16[:, 128 * h:128 * (h + 1)], rrT[:],
                start=True, stop=True), r=[ssel16, rrT], w=[pN])
            S.step("vector", lambda v: v.tensor_copy(pNs[:], pN[:]),
                   r=[pN], w=[pNs], fsz=128)
            S.step("vector", lambda v, h=h, b3=b3: v.tensor_tensor(
                out=sout[b3][:], in0=pA[:, 128 * h:128 * (h + 1)],
                in1=pNs[:], op=ALU.mult), r=[pA, pNs], w=[sout[b3]],
                fsz=128)
            dma(S, "gpsimd", sp_o[128 * h:128 * (h + 1),
                                  128 * t:128 * (t + 1)], sout[b3][:],
                r=[sout[b3]], w=[OUT])
    return S.finalize()


def emit_rope_tm(S, psrc, col0, nh, dst, dcol0, scos, ssin, t, ta, tb):
    pq = psrc[:, col0:col0 + 128 * nh].rearrange("p (h two d) -> p h two d",
                                                 two=2, d=64)
    q1, q2 = pq[:, :, 0, :], pq[:, :, 1, :]
    cosb = scos[:, t, :].unsqueeze(1).broadcast_to([128, nh, 64])
    sinb = ssin[:, t, :].unsqueeze(1).broadcast_to([128, nh, 64])
    va = ta[:].rearrange("p (h d) -> p h d", d=64)[:, 0:nh, :]
    vb = tb[:].rearrange("p (h d) -> p h d", d=64)[:, 0:nh, :]
    ot = dst[:, dcol0:dcol0 + 128 * nh].rearrange("p (h two d) -> p h two d",
                                                  two=2, d=64)
    fs = nh * 64
    S.step("vector", lambda v: v.tensor_tensor(out=va, in0=q1, in1=cosb,
                                               op=ALU.mult),
           r=[psrc, scos], w=[ta], fsz=fs)
    S.step("vector", lambda v: v.tensor_tensor(out=vb, in0=q2, in1=sinb,
                                               op=ALU.mult),
           r=[psrc, ssin], w=[tb], fsz=fs)
    S.step("vector", lambda v: v.tensor_tensor(out=ot[:, :, 0, :], in0=va,
                                               in1=vb, op=ALU.subtract),
           r=[ta, tb], w=[dst], fsz=fs)
    S.step("vector", lambda v: v.tensor_tensor(out=va, in0=q1, in1=sinb,
                                               op=ALU.mult),
           r=[psrc, ssin], w=[ta], fsz=fs)
    S.step("vector", lambda v: v.tensor_tensor(out=vb, in0=q2, in1=cosb,
                                               op=ALU.mult),
           r=[psrc, scos], w=[tb], fsz=fs)
    S.step("vector", lambda v: v.tensor_tensor(out=ot[:, :, 1, :], in0=va,
                                               in1=vb, op=ALU.add),
           r=[ta, tb], w=[dst], fsz=fs)



def _rope_tables():
    half = 64
    inv = 1.0 / (10000.0 ** (np.arange(half, dtype=np.float64) / half))
    t = np.arange(S, dtype=np.float64)
    f = np.outer(t, inv)
    return np.cos(f).astype(np.float32), np.sin(f).astype(np.float32)


def _run(name, builder, in_maps):
    if name not in _BUILT:
        _BUILT[name] = builder()
    res = run_bass_kernel_spmd(_BUILT[name], in_maps,
                               core_ids=list(range(NCORES)))
    return res.results


def _shard_rows(a):
    return [np.ascontiguousarray(a[PC * c:PC * (c + 1)]) for c in range(NCORES)]


def _gather_rows(results, key):
    return np.concatenate([r[key] for r in results], axis=0)


def run_ln0(x_flat):
    ins = [dict(x=s) for s in _shard_rows(x_flat)]
    return _gather_rows(_run("ln0", build_ln, ins), "xn")


def run_qkv0(xn, wqkv):
    cosT, sinT = _rope_tables()
    cos_full = np.concatenate([cosT] * B, axis=0)
    sin_full = np.concatenate([sinT] * B, axis=0)
    wq = np.ascontiguousarray(wqkv.astype(np.float32))
    ident = np.eye(128, dtype=BF)
    ins = []
    for c in range(NCORES):
        ins.append(dict(xn=np.ascontiguousarray(xn[PC * c:PC * (c + 1)]),
                        wqkv=wq,
                        cost=np.ascontiguousarray(cos_full[PC * c:PC * (c + 1)]),
                        sint=np.ascontiguousarray(sin_full[PC * c:PC * (c + 1)]),
                        identb=ident))
    res = _run("qkv0", build_qkv0, ins)
    return (_gather_rows(res, "q"), _gather_rows(res, "k"),
            _gather_rows(res, "v"))


def run_combine0(x_flat, sT_cores, gT_cores):
    ident = np.eye(128, dtype=BF)
    ins = [dict(x=a, sT=b_, gT=c_, identb=ident) for a, b_, c_ in
           zip(_shard_rows(x_flat), sT_cores, gT_cores)]
    res = _run("combine0", lambda: build_combine0(False), ins)
    return _gather_rows(res, "x1"), _gather_rows(res, "xn1")


def run_combine1(x1, sT_cores, gT_cores):
    ident = np.eye(128, dtype=BF)
    ins = [dict(x=a, sT=b_, gT=c_, identb=ident) for a, b_, c_ in
           zip(_shard_rows(x1), sT_cores, gT_cores)]
    res = _run("combine1", lambda: build_combine0(True), ins)
    return _gather_rows(res, "out")


def run_proj_fm(name, aT_cores, w):
    wf = np.ascontiguousarray(w.astype(np.float32))
    ins = [dict(aT=a, w=wf) for a in aT_cores]
    res = _run(name, build_proj_fm, ins)
    return [r["oT"] for r in res]


def _dup_transpose(packed):
    return np.ascontiguousarray(np.repeat(packed.T, 2, axis=0).astype(np.int32))


def run_dequant1(gpk, gsc, upk, usc, dpk, dsc):
    gT = _dup_transpose(gpk)
    uT = _dup_transpose(upk)
    dT = _dup_transpose(dpk)
    gscT = np.ascontiguousarray(gsc.T.astype(np.float32))
    uscT = np.ascontiguousarray(usc.T.astype(np.float32))
    dscT = np.ascontiguousarray(dsc.T.astype(np.float32))
    shifts = np.tile(np.array([0, 4], np.int32), 64)[:, None].copy()
    sel = np.zeros((4, 128), np.float32)
    for i in range(4):
        sel[i, 32 * i:32 * (i + 1)] = 1.0
    ins = []
    for c in range(NCORES):
        ins.append(dict(
            gpk=np.ascontiguousarray(gT[256 * c:256 * (c + 1)]),
            upk=np.ascontiguousarray(uT[256 * c:256 * (c + 1)]),
            dpk=np.ascontiguousarray(dT[2048 * c:2048 * (c + 1)]),
            gsc=np.ascontiguousarray(gscT[8 * c:8 * (c + 1)]),
            usc=np.ascontiguousarray(uscT[8 * c:8 * (c + 1)]),
            dsc=np.ascontiguousarray(dscT[64 * c:64 * (c + 1)]),
            shifts=shifts, sel=sel))
    res = _run("dequant1", build_dequant1, ins)
    gw = np.concatenate([r["gw"] for r in res], axis=0)
    uw = np.concatenate([r["uw"] for r in res], axis=0)
    dw = np.concatenate([r["dw"] for r in res], axis=0)
    return gw, uw, dw



def _sel16():
    s = np.zeros((16, 2048), np.float32)
    for h in range(16):
        s[h, 128 * h:128 * (h + 1)] = 1.0
    return s


def _sel8():
    s = np.zeros((8, 1024), np.float32)
    for e in range(8):
        s[e, 128 * e:128 * (e + 1)] = 1.0
    return s


def run_attn0(q, k, v, wo):
    ident = np.eye(128, dtype=BF)
    identf = np.eye(128, dtype=np.float32)
    iotak = np.tile(np.arange(SEQ, dtype=np.float32)[None, :], (128, 1))
    sel16 = _sel16()
    ins = []
    for c in range(NCORES):
        bb, h = c // 2, c % 2
        qpos = (2048 * h + 128 * np.arange(16)[None, :]
                + np.arange(128)[:, None]).astype(np.float32)
        ins.append(dict(q=np.ascontiguousarray(q[PC * c:PC * (c + 1)]),
                        k=np.ascontiguousarray(k[SEQ * bb:SEQ * (bb + 1)]),
                        v=np.ascontiguousarray(v[SEQ * bb:SEQ * (bb + 1)]),
                        identb=ident, identf=identf, qpos=qpos,
                        iotak=iotak, sel16=sel16))
    res = _run("attn0", build_attn0, ins)
    spre = [r["spre"] for r in res]
    sT = run_proj_fm("projfm", spre, wo)
    return sT


def run_moe0(xn, x_flat, gate_w, eg, eu, ed, sg, su, sd):
    ident = np.eye(128, dtype=BF)
    identf = np.eye(128, dtype=np.float32)
    prep = dict(
        gwT=np.ascontiguousarray(gate_w.T.astype(np.float32)),
        egT=np.ascontiguousarray(eg.transpose(0, 2, 1).astype(np.float32)),
        euT=np.ascontiguousarray(eu.transpose(0, 2, 1).astype(np.float32)),
        edT=np.ascontiguousarray(ed.transpose(0, 2, 1).astype(np.float32)),
        sgT=np.ascontiguousarray(sg.T.astype(np.float32)),
        suT=np.ascontiguousarray(su.T.astype(np.float32)),
        sdT=np.ascontiguousarray(sd.T.astype(np.float32)),
        identb=ident, identf=identf, sel8=_sel8())
    ins = [dict(xn=np.ascontiguousarray(xn[PC * c:PC * (c + 1)]),
                x=np.ascontiguousarray(x_flat[PC * c:PC * (c + 1)]), **prep)
           for c in range(NCORES)]
    res = _run("moe0", build_moe0, ins)
    return [r["gT"] for r in res]


_ROPE_PERM = np.concatenate([np.arange(0, 128, 2), np.arange(1, 128, 2)])


def run_attn1(xn1, wq, wk, wv, sink):
    ident = np.eye(128, dtype=BF)
    identf = np.eye(128, dtype=np.float32)
    sel16 = _sel16()
    cosT, sinT = _rope_tables()
    wqp = wq.astype(np.float32).reshape(D, 16, 128)[:, :, _ROPE_PERM]
    wqp = np.ascontiguousarray(wqp.reshape(D, D))
    wkp = wk.astype(np.float32).reshape(D, 4, 128)[:, :, _ROPE_PERM]
    wkp = wkp.reshape(D, 512)
    wkv = np.ascontiguousarray(
        np.concatenate([wkp, wv.astype(np.float32)], axis=1))
    sinkb = np.ascontiguousarray(
        np.tile(np.asarray(sink, np.float32).reshape(1, 16), (128, 1)))
    ins = []
    for c in range(NCORES):
        bb, h = c // 2, c % 2
        start = 2048 * h
        lo = start - 64
        xseq = xn1[SEQ * bb:SEQ * (bb + 1)]
        xh = np.zeros((2176, D), dtype=BF)
        s0, s1 = max(lo, 0), min(lo + 2176, SEQ)
        xh[s0 - lo:s1 - lo] = xseq[s0:s1]
        qpos_rows = start + np.arange(PC)
        kpos_rows = np.clip(lo + np.arange(2176), 0, SEQ - 1)
        combm = np.zeros((16, 128, 256), np.float32)
        p = np.arange(128)[:, None]
        j = np.arange(256)[None, :]
        band_bad = (j - p < 0) | (j - p > 128)
        for t in range(16):
            kl = 128 * t + j - 64 + start
            valid = (kl >= 0) & (kl < SEQ)
            combm[t] = np.where(band_bad | (~valid), -1e9, 0.0)
        ins.append(dict(xh=xh, wq=wqp, wkv=wkv, sinkb=sinkb,
                        cosq=np.ascontiguousarray(cosT[qpos_rows]),
                        sinq=np.ascontiguousarray(sinT[qpos_rows]),
                        cosk=np.ascontiguousarray(cosT[kpos_rows]),
                        sinkt=np.ascontiguousarray(sinT[kpos_rows]),
                        combm=combm, identb=ident, identf=identf,
                        sel16=sel16))
    res = _run("attn1", build_attn1, ins)
    return [r["spre"] for r in res]


def run_ffn1(xn1, gw, uw, dw):
    ident = np.eye(128, dtype=BF)
    ins = [dict(xn1=np.ascontiguousarray(xn1[PC * c:PC * (c + 1)]),
                gw=gw, uw=uw, dw=dw, identb=ident) for c in range(NCORES)]
    res = _run("ffn1", build_ffn1, ins)
    return [r["g1T"] for r in res]


def kernel(**inputs):
    x = np.asarray(inputs["L0_attn_hidden_states"], np.float32)
    xf = np.ascontiguousarray(x.reshape(-1, D))
    xn = run_ln0(xf)
    q, k, v = run_qkv0(xn, np.asarray(inputs["L0_attn_Wqkv"]))
    sT = run_attn0(q, k, v, np.asarray(inputs["L0_attn_Wo"]))
    gT = run_moe0(xn, xf, np.asarray(inputs["L0_ffn_gate_w"]),
                  np.asarray(inputs["L0_ffn_expert_gate"]),
                  np.asarray(inputs["L0_ffn_expert_up"]),
                  np.asarray(inputs["L0_ffn_expert_down"]),
                  np.asarray(inputs["L0_ffn_shared_gate"]),
                  np.asarray(inputs["L0_ffn_shared_up"]),
                  np.asarray(inputs["L0_ffn_shared_down"]))
    x1, xn1 = run_combine0(xf, sT, gT)
    spre1 = run_attn1(xn1, np.asarray(inputs["L1_attn_wq"]),
                      np.asarray(inputs["L1_attn_wk"]),
                      np.asarray(inputs["L1_attn_wv"]),
                      np.asarray(inputs["L1_attn_sink"]))
    s1T = run_proj_fm("projfm", spre1, np.asarray(inputs["L1_attn_wo"]))
    gwq, uwq, dwq = run_dequant1(
        np.asarray(inputs["L1_ffn_gate_W_packed"]),
        np.asarray(inputs["L1_ffn_gate_scales"]),
        np.asarray(inputs["L1_ffn_up_W_packed"]),
        np.asarray(inputs["L1_ffn_up_scales"]),
        np.asarray(inputs["L1_ffn_down_W_packed"]),
        np.asarray(inputs["L1_ffn_down_scales"]))
    g1T = run_ffn1(xn1, gwq, uwq, dwq)
    out = run_combine1(x1, s1T, g1T)
    return out.reshape(B, S, D)
